# version 01
# baseline (speedup 1.0000x reference)
import sys
sys.path.insert(0, '/opt/trn_rl_repo')

import numpy as np
import ml_dtypes
import concourse.bass as bass
import concourse.mybir as mybir
from concourse.alu_op_type import AluOpType
from concourse.bass_utils import run_bass_kernel_spmd
from concourse import dve_ops
from concourse.dve_op import DveOp
from concourse.dve_spec import Spec, Src0, C0, C1, C2, lower as dve_lower

F32 = mybir.dt.float32
BF16 = mybir.dt.bfloat16
AF = mybir.ActivationFunctionType
AX = mybir.AxisListType
OP = AluOpType

P = 128
D = 2048
S = 4096
NCH = 8
I_FFN = 5632
NQH, NKV = 16, 4
SCALE = float(1.0 / np.sqrt(128.0))
SHIFT = -8.0
EPS_LN = 1e-5
MAGIC = 12582912.0

_ROUND_SPEC = Spec(body=(((Src0 * C0) + C2) - C2) * C1)

def _round_ref(in0, in1, s0, s1, imm2):
    return np.round(in0 * s0) * s1

def _register_round_op():
    for op in dve_ops.OPS:
        if op.name == "ANT_ROUND_SCALE":
            return op
    shas = {}
    for ver in ("v3", "v4"):
        try:
            shas[ver] = dve_lower(_ROUND_SPEC, ver=ver, perf_en=False,
                                  name="ANT_ROUND_SCALE").sha(ver)
        except Exception:
            pass
    op = DveOp("ANT_ROUND_SCALE", _ROUND_SPEC, subdim=False, uops_sha=shas,
               reference=_round_ref)
    dve_ops.OPS.append(op)
    dve_ops._SUB_OPCODE_FOR_NAME["ANT_ROUND_SCALE"] = \
        dve_ops._CUSTOM_DVE_ROW_BASE + len(dve_ops.OPS) - 1
    dve_ops.CUSTOM_DVE_SPECS["ANT_ROUND_SCALE"] = _ROUND_SPEC
    dve_ops.CUSTOM_DVE_REFS["ANT_ROUND_SCALE"] = _round_ref
    return op

ROUND_OP = _register_round_op()

ENGS = ("sync", "scalar", "vector", "tensor", "gpsimd")
ZERO = ("z", 0)

class Plan:
    def __init__(self):
        self.plan = {e: [] for e in ENGS}
        self.cnt = {"z": 0}
        self.hw = {}

    def wait(self, eng, tok):
        sem, val = tok
        if sem == "z" or val <= 0:
            return
        if self.hw.get((eng, sem), -1) >= val:
            return
        self.hw[(eng, sem)] = val
        self.plan[eng].append(("w", sem, val))

    def do(self, eng, fn, inc=None, n=1, flush=False):
        if flush and inc is None:
            inc = {"vector": "sdv", "scalar": "sact", "tensor": "spe",
                   "gpsimd": "sg"}[eng]
        tok = ZERO
        if inc is not None:
            v = self.cnt.get(inc, 0) + n
            self.cnt[inc] = v
            tok = (inc, v)
        self.plan[eng].append(("i", fn, inc, n))
        if flush:
            self.wait(eng, tok)
        return tok

    def dma(self, eng, dst, src, semn, slow=False):
        return self.do(eng, lambda e, d=dst, s=src, sl=slow: e.dma_start(
            out=d, in_=s, allow_slow_non_contiguous=sl), inc=semn, n=16)

def tmax(*toks):
    best = {}
    for s, v in toks:
        if s != "z" and v > best.get(s, -1):
            best[s] = v
    return [(s, v) for s, v in best.items()]


def build_program(debug=False):
    nc = bass.Bass("TRN2", target_bir_lowering=False, debug=False)

    x_hbm = nc.dram_tensor("x", [S, D], F32, kind="ExternalInput").ap()
    wq_hbm = nc.dram_tensor("wq", [D, D], F32, kind="ExternalInput").ap()
    wk_hbm = nc.dram_tensor("wk", [D, 512], F32, kind="ExternalInput").ap()
    wv_hbm = nc.dram_tensor("wv", [D, 512], F32, kind="ExternalInput").ap()
    wo_hbm = nc.dram_tensor("wo", [D, D], F32, kind="ExternalInput").ap()
    w1_hbm = nc.dram_tensor("w1", [2 * I_FFN, D], F32, kind="ExternalInput").ap()
    w2_hbm = nc.dram_tensor("w2", [D, I_FFN], F32, kind="ExternalInput").ap()
    cos_hbm = nc.dram_tensor("cosd", [P, S], F32, kind="ExternalInput").ap()
    sin_hbm = nc.dram_tensor("msin", [P, S], F32, kind="ExternalInput").ap()
    mask_hbm = nc.dram_tensor("maskt", [P, 512], F32, kind="ExternalInput").ap()
    id_hbm = nc.dram_tensor("ident", [P, P], F32, kind="ExternalInput").ap()
    out_hbm = nc.dram_tensor("out", [1024, D], F32, kind="ExternalOutput").ap()
    attn_dram = nc.dram_tensor(
        "attn_scratch", [1024, D], BF16,
        kind=("ExternalOutput" if debug else "Internal")).ap()
    if debug:
        dbg_k = nc.dram_tensor("d_k", [P, NKV * S], BF16, kind="ExternalOutput").ap()
        dbg_v = nc.dram_tensor("d_v", [P, NKV * S], BF16, kind="ExternalOutput").ap()
        dbg_ctx = nc.dram_tensor("d_ctx", [P, 16 * 1024], BF16, kind="ExternalOutput").ap()
        dbg_xnq = nc.dram_tensor("d_xnq", [P, 16 * 1024], BF16, kind="ExternalOutput").ap()
        dbg_hT = nc.dram_tensor("d_hT", [P, 44 * 1024], BF16, kind="ExternalOutput").ap()
        dbg_qh = nc.dram_tensor("d_qh", [P, 2048], BF16, kind="ExternalOutput").ap()
        dbg_sc = nc.dram_tensor("d_sc", [P, 48], F32, kind="ExternalOutput").ap()
        dbg_xn = nc.dram_tensor("d_xn", [P, 16 * 1024], BF16, kind="ExternalOutput").ap()
        dbg_st = nc.dram_tensor("d_st", [P, 8], F32, kind="ExternalOutput").ap()
        dbg_t28 = nc.dram_tensor("d_t28", [P, D], BF16, kind="ExternalOutput").ap()
        dbg_sx28 = nc.dram_tensor("d_sx28", [P, D], F32, kind="ExternalOutput").ap()
        dbg_pk = nc.dram_tensor("d_pk", [P, 512], F32, kind="ExternalOutput").ap()
        dbg_kev = nc.dram_tensor("d_kev", [P, 2 * 512], BF16, kind="ExternalOutput").ap()
        dbg_cs = nc.dram_tensor("d_cs", [P, 2 * 512], BF16, kind="ExternalOutput").ap()
        dbg_rp = nc.dram_tensor("d_rp", [P, 2 * 512], BF16, kind="ExternalOutput").ap()
        dbg_kb0 = nc.dram_tensor("d_kb0", [P, 512], BF16, kind="ExternalOutput").ap()
        dbg_kbt = nc.dram_tensor("d_kbt", [P, 4 * 512], BF16, kind="ExternalOutput").ap()
        dbg_h0 = nc.dram_tensor("d_h0", [P, 2 * 1024], F32, kind="ExternalOutput").ap()
        dbg_pt = nc.dram_tensor("d_pt", [P, 1024], F32, kind="ExternalOutput").ap()
        dbg_lnz = nc.dram_tensor("d_lnz", [1, 2 * 1024], F32, kind="ExternalOutput").ap()

    p = Plan()
    SX, SG, SM, SPE, SACT, SDV, SOUT = "sx", "sg", "sm", "spe", "sact", "sdv", "sout"

    sb_mgrs = {}
    def sb(name, shape, dt, side=None):
        m = nc.sbuf_tensor(name, shape, dt, side=side)
        h = m.__enter__()
        sb_mgrs[name] = m
        return h
    def sbfree(*names):
        for n in names:
            sb_mgrs.pop(n).__exit__(None, None, None)
    ps_mgrs = {}
    def ps(name, shape, dt):
        m = nc.psum_tensor(name, shape, dt)
        h = m.__enter__()
        ps_mgrs[name] = m
        return h
    def psfree(*names):
        for n in names:
            ps_mgrs.pop(n).__exit__(None, None, None)

    ident_bf = sb("ident_bf", [P, P], BF16)
    ident_f = sb("ident_f", [P, P], F32)
    ones128 = sb("ones128", [P, 1], BF16)
    onesr = sb("onesr", [1, P], F32)
    sxq_scale = sb("sxq_scale", [P, 8], F32)
    sxq_rscale = sb("sxq_rscale", [P, 8], F32)
    shq_amax = sb("shq_amax", [P, 8], F32)
    shq_scale = sb("shq_scale", [P, 8], F32)
    shq_rscale = sb("shq_rscale", [P, 8], F32)
    amax_t = sb("amax_t", [P, 1], F32)
    wsc_t = sb("wsc_t", [P, 1], F32)
    wrs_t = sb("wrs_t", [P, 1], F32)
    w2_amax = sb("w2_amax", [P, 16], F32)
    w2_sc = sb("w2_sc", [P, 16], F32)
    w2_rs = sb("w2_rs", [P, 16], F32)
    kbuf = sb("kbuf", [P, NKV * S], BF16)
    vbuf = sb("vbuf", [P, NKV * S], BF16)
    cosq_bf = sb("cosq_bf", [P, 1024], BF16)
    sinq_bf = sb("sinq_bf", [P, 1024], BF16)
    maskt_bf = sb("maskt_bf", [P, 512], BF16)
    rope1 = sb("rope1", [P, 1024], BF16)
    rope2 = sb("rope2", [P, 1024], BF16)
    xnT_own = sb("xnT_own", [P, 16 * 1024], BF16, side="right")

    t_id = p.dma("scalar", ident_f.ap()[:, :], id_hbm, "dmid")
    t_o128 = p.do("gpsimd", lambda e: e.memset(ones128.ap()[:, :], 1.0), inc=SG)
    t_ones = p.do("gpsimd", lambda e: e.memset(onesr.ap()[:, :], 1.0), inc=SG)
    p.wait("vector", t_id)
    t_idbf = p.do("vector", lambda e: e.tensor_copy(ident_bf.ap()[:, :], ident_f.ap()[:, :]), inc=SDV)

    maskt_f = sb("maskt_f", [P, 512], F32)
    t_mask = p.dma("scalar", maskt_f.ap()[:, :], mask_hbm, "dmask")
    p.wait("vector", t_mask)
    t_maskbf = p.do("vector", lambda e: e.tensor_copy(maskt_bf.ap()[:, :], maskt_f.ap()[:, :]), inc=SDV)
    sx = [sb(f"sx{i}", [P, D], F32) for i in range(2)]
    sq_scr = sb("sq_scr", [P, D], F32)
    xn_bf = [sb(f"xn_bf{i}", [P, D], BF16) for i in range(2)]
    xnT_ch = sb("xnT_ch", [P, 16 * 512], BF16)
    wkbf = sb("wkbf", [P, 16 * 512], BF16)
    wvbf = sb("wvbf", [P, 16 * 512], BF16)
    kevac = sb("kevac", [P, 512], BF16)
    kevac_sw = sb("kevac_sw", [P, 512], BF16)
    cos_bf = sb("cos_bf", [P, 512], BF16)
    sin_bf = sb("sin_bf", [P, 512], BF16)
    stats = [{k: sb(f"st_{k}{i}", [P, 1], F32) for k in ("sum", "sq", "t", "rstd")}
             for i in range(2)]

    pp_xt = [ps(f"pp_xt{i}", [P, D], BF16) for i in range(2)]
    pk = [ps(f"pk{i}", [P, 512], F32) for i in range(2)]
    pv = [ps(f"pv{i}", [P, 512], F32) for i in range(2)]

    t_wk = p.dma("gpsimd", wkbf.ap().rearrange("p (j c) -> p j c", j=16),
                 wk_hbm.rearrange("(j p) c -> p j c", p=P), "dwk")
    t_wv = p.dma("gpsimd", wvbf.ap().rearrange("p (j c) -> p j c", j=16),
                 wv_hbm.rearrange("(j p) c -> p j c", p=P), "dwv")

    sx_free = [ZERO, ZERO]
    ppxt_free = [ZERO, ZERO]
    xn_free = [ZERO, ZERO]
    pk_free = [ZERO, ZERO]
    pv_free = [ZERO, ZERO]
    kevac_free = ZERO
    stats_free = [ZERO, ZERO]
    cbf_free = ZERO

    t_x = [ZERO] * 32
    xq_tok = ZERO
    t_kv_prev = ZERO
    t_xnT_last = ZERO

    for a in range(NCH):
        p.wait("gpsimd", cbf_free)
        p.dma("gpsimd", cos_bf.ap()[:, :], cos_hbm[:, a * 512:(a + 1) * 512], "dcs")
        t_csbf = p.dma("gpsimd", sin_bf.ap()[:, :], sin_hbm[:, a * 512:(a + 1) * 512], "dcs")

        ev_toks = []
        for u in range(4):
            i = a * 4 + u
            bi = i % 2
            p.wait("sync", sx_free[bi])
            t_x[i] = p.dma("sync", sx[bi].ap()[:, :], x_hbm[i * P:(i + 1) * P, :], f"dx{bi}")
            st = stats[bi]
            p.wait("vector", t_x[i])
            p.wait("vector", stats_free[bi])
            p.do("vector", lambda e, b=bi: e.tensor_reduce(
                stats[b]["sum"].ap()[:, :], sx[b].ap()[:, :], axis=AX.X, op=OP.add),
                flush=True)
            t_sq = p.do("vector", lambda e, b=bi: e.tensor_tensor_reduce(
                out=sq_scr.ap()[:, :], in0=sx[b].ap()[:, :], in1=sx[b].ap()[:, :],
                scale=1.0, scalar=0.0, op0=OP.mult, op1=OP.add,
                accum_out=stats[b]["sq"].ap()[:, :]), inc=SDV, flush=True)
            p.do("vector", lambda e, b=bi: e.tensor_scalar(
                stats[b]["sum"].ap()[:, :], stats[b]["sum"].ap()[:, :], 1.0 / D, None, op0=OP.mult),
                flush=True)
            p.do("vector", lambda e, b=bi: e.tensor_scalar(
                stats[b]["sq"].ap()[:, :], stats[b]["sq"].ap()[:, :], 1.0 / D, None, op0=OP.mult),
                flush=True)
            p.do("vector", lambda e, b=bi: e.tensor_tensor(
                stats[b]["t"].ap()[:, :], stats[b]["sum"].ap()[:, :], stats[b]["sum"].ap()[:, :], op=OP.mult),
                flush=True)
            t_var = p.do("vector", lambda e, b=bi: e.tensor_tensor(
                stats[b]["sq"].ap()[:, :], stats[b]["sq"].ap()[:, :], stats[b]["t"].ap()[:, :],
                op=OP.subtract), inc=SDV, flush=True)
            p.wait("scalar", t_var)
            t_sqr = p.do("scalar", lambda e, b=bi: e.activation(
                stats[b]["t"].ap()[:, :], stats[b]["sq"].ap()[:, :], AF.Sqrt,
                bias=np.float32(EPS_LN)), inc=SACT, flush=True)
            p.wait("vector", t_sqr)
            p.do("vector", lambda e, b=bi: e.reciprocal(
                stats[b]["rstd"].ap()[:, :], stats[b]["t"].ap()[:, :]), flush=True)
            p.do("vector", lambda e, b=bi: e.tensor_tensor(
                stats[b]["t"].ap()[:, :], stats[b]["sum"].ap()[:, :], stats[b]["rstd"].ap()[:, :],
                op=OP.mult), flush=True)
            t_nm = p.do("vector", lambda e, b=bi: e.tensor_scalar(
                stats[b]["t"].ap()[:, :], stats[b]["t"].ap()[:, :], -1.0, None, op0=OP.mult), inc=SDV,
                flush=True)
            p.wait("vector", t_nm)
            p.wait("vector", xn_free[bi])
            t_xn = p.do("vector", lambda e, b=bi: e.tensor_scalar(
                xn_bf[b].ap()[:, :], sx[b].ap()[:, :],
                stats[b]["rstd"].ap()[:, :], stats[b]["t"].ap()[:, :],
                op0=OP.mult, op1=OP.add), inc=SDV, flush=True)
            sx_free[bi] = t_xn
            stats_free[bi] = t_xn
            if debug and i == 28:
                p.wait("sync", t_xn)
                p.dma("sync", dbg_st[:, 0:1], stats[bi]["sum"].ap()[:, :], "ddbg", slow=True)
                p.dma("sync", dbg_st[:, 1:2], stats[bi]["sq"].ap()[:, :], "ddbg", slow=True)
                p.dma("sync", dbg_st[:, 2:3], stats[bi]["t"].ap()[:, :], "ddbg", slow=True)
                p.dma("sync", dbg_st[:, 3:4], stats[bi]["rstd"].ap()[:, :], "ddbg", slow=True)
                p.dma("sync", dbg_sx28, sx[bi].ap()[:, :], "ddbg")
                t_d28 = p.dma("sync", dbg_t28, xn_bf[bi].ap()[:, :], "ddbg")
                p.wait("sync", t_d28)
                p.wait("vector", t_d28)
                p.wait("scalar", t_d28)
            if u == 0:
                p.wait("vector", t_xn)
                p.do("vector", lambda e, b=bi, aa=a: e.tensor_reduce(
                    sxq_scale.ap()[:, aa:aa + 1], xn_bf[b].ap()[:, :], axis=AX.X,
                    op=OP.max, apply_absolute_value=True), flush=True)
                p.do("vector", lambda e, aa=a: e.tensor_scalar(
                    sxq_scale.ap()[:, aa:aa + 1], sxq_scale.ap()[:, aa:aa + 1],
                    1.0 / 127.0, 1e-8, op0=OP.mult, op1=OP.max), flush=True)
                xq_tok = p.do("vector", lambda e, aa=a: e.reciprocal(
                    sxq_rscale.ap()[:, aa:aa + 1], sxq_scale.ap()[:, aa:aa + 1]), inc=SDV,
                    flush=True)
            p.wait("tensor", t_xn)
            p.wait("tensor", t_idbf)
            p.wait("tensor", ppxt_free[bi])
            t_tr = ZERO
            for j in range(16):
                t_tr = p.do("tensor", lambda e, b=bi, jj=j: e.transpose(
                    pp_xt[b].ap()[:, jj * P:(jj + 1) * P],
                    xn_bf[b].ap()[:, jj * P:(jj + 1) * P], ident_bf.ap()[:, :]),
                    inc=(SPE if j == 15 else None))
            xn_free[bi] = t_tr
            p.wait("vector", t_tr)
            p.wait("vector", t_kv_prev)
            dst = xnT_ch.ap().rearrange("p (j c) -> p j c", j=16)[:, :, u * P:(u + 1) * P]
            t_ev = p.do("vector", lambda e, b=bi, d=dst: e.tensor_copy(
                d, pp_xt[b].ap().rearrange("p (j c) -> p j c", j=16)), inc=SDV,
                flush=True)
            ppxt_free[bi] = t_ev
            ev_toks.append(t_ev)
            if u == 0:
                dsto = xnT_own.ap().rearrange("p (j c) -> p j c", j=16)[:, :, a * P:(a + 1) * P]
                srco = xnT_ch.ap().rearrange("p (j c) -> p j c", j=16)[:, :, 0:P]
                p.do("vector", lambda e, d=dsto, s=srco: e.tensor_copy(d, s), inc=SDV,
                     flush=True)
                p.wait("vector", t_csbf)
                p.do("vector", lambda e, aa=a: e.tensor_copy(
                    cosq_bf.ap()[:, aa * P:(aa + 1) * P], cos_bf.ap()[:, 0:P]))
                p.do("vector", lambda e, aa=a: e.tensor_copy(
                    sinq_bf.ap()[:, aa * P:(aa + 1) * P], sin_bf.ap()[:, 0:P]), inc=SDV,
                    flush=True)

        for tok in tmax(*ev_toks):
            p.wait("tensor", tok)
        p.wait("tensor", t_wk)
        rope_tok = ZERO
        for kc in range(NKV):
            bi = kc % 2
            p.wait("tensor", pk_free[bi])
            t_mm = ZERO
            for j in range(16):
                t_mm = p.do("tensor", lambda e, b=bi, jj=j, k=kc: e.matmul(
                    pk[b].ap()[:, :],
                    wkbf.ap()[:, jj * 512 + k * P: jj * 512 + (k + 1) * P],
                    xnT_ch.ap()[:, jj * 512:(jj + 1) * 512],
                    start=(jj == 0), stop=(jj == 15)),
                    inc=(SPE if j == 15 else None))
            p.wait("scalar", t_mm)
            p.wait("scalar", kevac_free)
            p.do("scalar", lambda e, b=bi: e.activation(
                kevac.ap()[:, :], pk[b].ap()[:, :], AF.Copy), flush=True)
            p.do("scalar", lambda e, b=bi: e.activation(
                kevac_sw.ap()[0:64, :], pk[b].ap()[64:128, :], AF.Copy), flush=True)
            t_kev = p.do("scalar", lambda e, b=bi: e.activation(
                kevac_sw.ap()[64:128, :], pk[b].ap()[0:64, :], AF.Copy), inc=SACT,
                flush=True)
            pk_free[bi] = t_kev
            if debug and a == 0 and kc == 0:
                dbg_sb = nc.sbuf_tensor("dbg_sb", [P, 512], F32)
                dbg_sbh = dbg_sb.__enter__()
                p.wait("vector", t_mm)
                t_dbgc = p.do("vector", lambda e, k=kc: e.tensor_copy(
                    dbg_sbh.ap()[:, :], pk[k % 2].ap()[:, :]), inc=SDV, flush=True)
                p.wait("sync", t_dbgc)
                p.wait("sync", t_kev)
                p.dma("sync", dbg_pk, dbg_sbh.ap()[:, :], "ddbg")
                p.dma("sync", dbg_kev[:, 0:512], kevac.ap()[:, :], "ddbg")
                p.dma("sync", dbg_cs[:, 0:512], cos_bf.ap()[:, :], "ddbg")
                p.dma("sync", dbg_cs[:, 512:1024], sin_bf.ap()[:, :], "ddbg")
                t_dk2 = p.dma("sync", dbg_kev[:, 512:1024], kevac_sw.ap()[:, :], "ddbg")
                p.wait("sync", t_dk2)
                p.wait("vector", t_dk2)
                p.wait("scalar", t_dk2)
                p.wait("tensor", t_dk2)
                dbg_sb.__exit__(None, None, None)
            p.wait("vector", t_kev)
            p.wait("vector", t_csbf)
            kdst = kbuf.ap()[:, kc * S + a * 512: kc * S + (a + 1) * 512]
            p.do("vector", lambda e: e.tensor_tensor(
                rope1.ap()[:, 0:512], kevac.ap()[:, :], cos_bf.ap()[:, :], op=OP.mult))
            p.do("vector", lambda e: e.tensor_tensor(
                rope2.ap()[:, 0:512], kevac_sw.ap()[:, :], sin_bf.ap()[:, :], op=OP.mult))
            rope_tok = p.do("vector", lambda e, d=kdst: e.tensor_tensor(
                d, rope1.ap()[:, 0:512], rope2.ap()[:, 0:512], op=OP.add), inc=SDV,
                flush=True)
            kevac_free = rope_tok
            if debug and a == 0 and kc == 0:
                p.wait("sync", rope_tok)
                p.dma("sync", dbg_rp[:, 0:512], rope1.ap()[:, 0:512], "ddbg")
                p.dma("sync", dbg_rp[:, 512:1024], rope2.ap()[:, 0:512], "ddbg")
                t_dk3 = p.dma("sync", dbg_kb0, kbuf.ap()[:, 0:512], "ddbg")
                p.wait("sync", t_dk3)
                p.wait("vector", t_dk3)
                p.wait("scalar", t_dk3)
                p.wait("tensor", t_dk3)
        cbf_free = rope_tok

        p.wait("tensor", t_wv)
        for u in range(4):
            bi = u % 2
            p.wait("tensor", pv_free[bi])
            t_mm = ZERO
            for j in range(16):
                t_mm = p.do("tensor", lambda e, b=bi, jj=j, uu=u: e.matmul(
                    pv[b].ap()[:, :],
                    xnT_ch.ap()[:, jj * 512 + uu * P: jj * 512 + (uu + 1) * P],
                    wvbf.ap()[:, jj * 512:(jj + 1) * 512],
                    start=(jj == 0), stop=(jj == 15)),
                    inc=(SPE if j == 15 else None))
            p.wait("scalar", t_mm)
            kt = a * 4 + u
            t_pv = ZERO
            for hv in range(NKV):
                t_pv = p.do("scalar", lambda e, b=bi, h=hv, k=kt: e.activation(
                    vbuf.ap()[:, h * S + k * P: h * S + (k + 1) * P],
                    pv[b].ap()[:, h * P:(h + 1) * P], AF.Copy),
                    inc=(SACT if hv == NKV - 1 else None),
                    flush=True)
            pv_free[bi] = t_pv
        t_kv_prev = t_mm

    sbfree(*[f"st_{k}{i}" for i in reversed(range(2)) for k in reversed(("sum", "sq", "t", "rstd"))],
           "sin_bf", "cos_bf", "kevac_sw", "kevac", "wvbf", "wkbf", "xnT_ch",
           "xn_bf1", "xn_bf0", "sq_scr", "sx1", "sx0", "maskt_f")
    psfree("pv1", "pv0", "pk1", "pk0", "pp_xt1", "pp_xt0")

    wobf = sb("wobf", [P, 8 * D], BF16)
    wqbf = [sb(f"wqbf{i}", [P, 16 * P], BF16) for i in range(2)]
    qh_buf = [sb(f"qh{i}", [P, 1024], BF16) for i in range(2)]
    qevac = sb("qevac", [P, 1024], BF16)
    qevac_sw = rope2
    ctxAll = sb("ctxAll", [P, 16 * 1024], BF16)
    pT_sb = [sb(f"pT{i}", [P, 512], BF16) for i in range(2)]
    lnz_sb = sb("lnz_sb", [1, 1024], F32)
    bz_sb = sb("bz_sb", [P, 1024], F32)
    rz_sb = bz_sb

    pq = ps("pq", [P, 1024], F32)
    pctx = ps("pctx", [P, 1024], F32)
    pz = ps("pz", [P, 1024], F32)
    pst = [ps(f"pst{i}", [P, 512], F32) for i in range(2)]

    if debug:
        p.wait("sync", (SDV, p.cnt[SDV]))
        t_dxn = p.dma("sync", dbg_xn, xnT_own.ap()[:, :], "ddbg")
        p.dma("sync", dbg_kbt[:, 0:512], kbuf.ap()[:, 0:512], "ddbg")
        p.wait("sync", t_dxn)
        p.wait("tensor", t_dxn)
        p.wait("vector", t_dxn)
        p.wait("scalar", t_dxn)
        p.wait("gpsimd", t_dxn)

    for eng in ("gpsimd",):
        p.wait(eng, (SPE, p.cnt[SPE]))
        p.wait(eng, (SDV, p.cnt[SDV]))
        p.wait(eng, (SACT, p.cnt[SACT]))
    p.wait("tensor", t_xnT_last)
    t_wo0 = p.dma("gpsimd", wobf.ap().rearrange("p (j c) -> p j c", j=8),
                  wo_hbm[0:1024, :].rearrange("(j p) c -> p j c", p=P), "dwo")

    t_wq = [ZERO, ZERO]
    t_wq[0] = p.dma("gpsimd", wqbf[0].ap().rearrange("p (j c) -> p j c", j=16),
                    wq_hbm[:, 0:P].rearrange("(j p) c -> p j c", p=P), "dwq0")

    pq_free = ZERO
    t_rope_prev = ZERO
    pst_free = [ZERO, ZERO]
    pT_free = [ZERO, ZERO]
    qh_free = [ZERO, ZERO]
    qproj_done = [ZERO] * NQH
    ctx_done = [ZERO] * NQH
    pcz_free = ZERO

    for h in range(NQH):
        kv = h % 4
        h_out = (h % 4) * 4 + (h // 4)
        bi = h % 2
        if h + 1 < NQH:
            if h >= 1:
                p.wait("gpsimd", qproj_done[h - 1])
            t_wq[(h + 1) % 2] = p.dma(
                "gpsimd", wqbf[(h + 1) % 2].ap().rearrange("p (j c) -> p j c", j=16),
                wq_hbm[:, (h + 1) * P:(h + 2) * P].rearrange("(j p) c -> p j c", p=P),
                f"dwq{(h + 1) % 2}")
        p.wait("tensor", t_wq[bi])
        p.wait("tensor", pq_free)
        t_mm = ZERO
        for half in range(2):
            for j in range(16):
                t_mm = p.do("tensor", lambda e, jj=j, hf=half, b=bi: e.matmul(
                    pq.ap()[:, hf * 512:(hf + 1) * 512],
                    wqbf[b].ap()[:, jj * P:(jj + 1) * P],
                    xnT_own.ap()[:, jj * 1024 + hf * 512: jj * 1024 + (hf + 1) * 512],
                    start=(jj == 0), stop=(jj == 15)),
                    inc=(SPE if (j == 15 and half == 1) else None))
        qproj_done[h] = t_mm
        p.wait("scalar", t_mm)
        p.wait("scalar", t_rope_prev)
        p.do("scalar", lambda e: e.activation(
            qevac.ap()[:, :], pq.ap()[:, :], AF.Copy), flush=True)
        p.do("scalar", lambda e: e.activation(
            qevac_sw.ap()[0:64, :], pq.ap()[64:128, :], AF.Copy), flush=True)
        pq_free = p.do("scalar", lambda e: e.activation(
            qevac_sw.ap()[64:128, :], pq.ap()[0:64, :], AF.Copy), inc=SACT,
            flush=True)
        p.wait("vector", pq_free)
        p.wait("vector", qh_free[bi])
        qdst = qh_buf[bi].ap()
        p.do("vector", lambda e: e.tensor_tensor(
            rope1.ap()[:, :], qevac.ap()[:, :], cosq_bf.ap()[:, :], op=OP.mult))
        p.do("vector", lambda e: e.tensor_tensor(
            rope2.ap()[:, :], rope2.ap()[:, :], sinq_bf.ap()[:, :], op=OP.mult))
        t_rope = p.do("vector", lambda e, d=qdst: e.tensor_tensor(
            d, rope1.ap()[:, :], rope2.ap()[:, :], op=OP.add), inc=SDV, flush=True)
        t_rope_prev = t_rope
        if debug and h == 0:
            p.wait("sync", t_rope)
            t_dq0 = p.dma("sync", dbg_qh[:, 0:1024], qh_buf[0].ap()[:, :], "ddbg")
            p.wait("sync", t_dq0)
            p.wait("tensor", t_dq0)
            p.wait("vector", t_dq0)
            p.wait("scalar", t_dq0)

        p.wait("tensor", t_rope)
        p.wait("tensor", pcz_free)
        p.wait("tensor", t_o128)
        pi = 0
        t_pv_last = ZERO
        for c in range(NCH):
            qoff = c * P
            spans = ([(qoff, 512), (512, 1024)] if qoff < 512 else [(qoff, 1024)])
            for u in range(4):
                kt = c * 4 + u
                klhs = kbuf.ap()[:, kv * S + kt * P: kv * S + (kt + 1) * P]
                vlhs = vbuf.ap()[:, kv * S + kt * P: kv * S + (kt + 1) * P]
                first = (c == 0 and u == 0)
                for (q0, q1) in spans:
                    cur = pi % 2
                    pi += 1
                    p.wait("tensor", pst_free[cur])
                    t_st = p.do("tensor", lambda e, kl=klhs, a0=q0, a1=q1, cc=cur, b=bi: e.matmul(
                        pst[cc].ap()[:, 0:a1 - a0], kl, qh_buf[b].ap()[:, a0:a1],
                        start=True, stop=True), inc=SPE)
                    p.wait("scalar", t_st)
                    p.wait("scalar", pT_free[cur])
                    t_exp = p.do("scalar", lambda e, a0=q0, a1=q1, cc=cur: e.activation(
                        pT_sb[cc].ap()[:, 0:a1 - a0], pst[cc].ap()[:, 0:a1 - a0],
                        AF.Exp, bias=np.float32(SHIFT), scale=np.float32(SCALE)),
                        inc=SACT, flush=True)
                    pst_free[cur] = t_exp
                    if debug and h == 0 and c == 0 and u == 0 and q0 == 0:
                        dbg_m2 = nc.sbuf_tensor("dbg_pst", [P, 512], F32)
                        dbg_ph2 = dbg_m2.__enter__()
                        p.wait("vector", t_exp)
                        t_dp = p.do("vector", lambda e, cc=cur: e.tensor_copy(
                            dbg_ph2.ap()[:, :], pst[cc].ap()[:, :]), inc=SDV, flush=True)
                        p.wait("sync", t_dp)
                        p.dma("sync", dbg_pt[:, 0:512], dbg_ph2.ap()[:, :], "ddbg")
                        t_dp2 = p.dma("gpsimd", dbg_pt[:, 512:1024], pT_sb[cur].ap()[:, :], "ddbg")
                        p.wait("sync", t_dp2)
                        p.wait("tensor", t_dp2)
                        p.wait("vector", t_dp2)
                        p.wait("scalar", t_dp2)
                        dbg_m2.__exit__(None, None, None)
                    use_tok = t_exp
                    if q0 == qoff:
                        p.wait("vector", t_exp)
                        p.wait("vector", t_maskbf)
                        use_tok = p.do("vector", lambda e, uu=u, cc=cur: e.tensor_tensor(
                            pT_sb[cc].ap()[:, 0:P], pT_sb[cc].ap()[:, 0:P],
                            maskt_bf.ap()[:, uu * P:(uu + 1) * P], op=OP.mult), inc=SDV,
                            flush=True)
                    p.wait("tensor", use_tok)
                    p.do("tensor", lambda e, a0=q0, a1=q1, cc=cur, fz=first: e.matmul(
                        pz.ap()[0:1, a0:a1], ones128.ap()[:, :],
                        pT_sb[cc].ap()[:, 0:a1 - a0],
                        start=fz, stop=False, skip_group_check=True))
                    t_pv_last = p.do("tensor", lambda e, vl=vlhs, a0=q0, a1=q1, cc=cur,
                                     fc=first: e.matmul(
                        pctx.ap()[:, a0:a1], vl, pT_sb[cc].ap()[:, 0:a1 - a0],
                        start=fc, stop=False, skip_group_check=True), inc=SPE)
                    pT_free[cur] = t_pv_last
        qh_free[bi] = t_pv_last
        p.wait("scalar", t_pv_last)
        t_ln = p.do("scalar", lambda e: e.activation(
            lnz_sb.ap()[:, :], pz.ap()[0:1, 0:1024], AF.Ln), inc=SACT, flush=True)
        t_rz = p.do("scalar", lambda e: e.activation(
            rz_sb.ap()[0:1, :], lnz_sb.ap()[:, :], AF.Exp, scale=np.float32(-1.0)),
            inc=SACT, flush=True)
        p.wait("tensor", t_rz)
        p.wait("tensor", t_ones)
        t_bz = ZERO
        for half in range(2):
            p.wait("tensor", pst_free[half])
            t_bz = p.do("tensor", lambda e, hf=half: e.matmul(
                pst[hf].ap()[:, 0:512], onesr.ap()[:, :],
                rz_sb.ap()[0:1, hf * 512:(hf + 1) * 512], start=True, stop=True),
                inc=SPE)
        p.wait("vector", t_bz)
        p.do("vector", lambda e: e.tensor_copy(bz_sb.ap()[:, 0:512], pst[0].ap()[:, :]))
        t_bzc = p.do("vector", lambda e: e.tensor_copy(
            bz_sb.ap()[:, 512:1024], pst[1].ap()[:, :]), inc=SDV, flush=True)
        t_ctx = p.do("vector", lambda e, ho=h_out: e.tensor_tensor(
            ctxAll.ap()[:, ho * 1024:(ho + 1) * 1024], pctx.ap()[:, :],
            bz_sb.ap()[:, :], op=OP.mult), inc=SDV, flush=True)
        ctx_done[h] = t_ctx
        if debug and h == 0:
            dbg_m = nc.sbuf_tensor("dbg_pctx", [P, 1024], F32)
            dbg_ph = dbg_m.__enter__()
            p.wait("vector", t_ctx)
            t_dc = p.do("vector", lambda e: e.tensor_copy(dbg_ph.ap()[:, :], pctx.ap()[:, :]),
                        inc=SDV, flush=True)
            p.wait("sync", t_dc)
            p.dma("sync", dbg_h0[:, 0:1024], dbg_ph.ap()[:, :], "ddbg")
            p.dma("sync", dbg_h0[:, 1024:2048], bz_sb.ap()[:, :], "ddbg")
            t_dl = p.dma("sync", dbg_lnz[:, 0:1024], lnz_sb.ap()[:, :], "ddbg")
            p.wait("sync", t_dl)
            p.wait("tensor", t_dl)
            p.wait("vector", t_dl)
            p.wait("scalar", t_dl)
            dbg_m.__exit__(None, None, None)
        if debug and h in (0, 7, 15):
            p.wait("sync", t_ctx)
            t_kbt = p.dma("sync", dbg_kbt[:, 512 * (1 + (0 if h == 0 else 1 if h == 7 else 2)):
                                          512 * (2 + (0 if h == 0 else 1 if h == 7 else 2))],
                          kbuf.ap()[:, 0:512], "ddbg")
            p.wait("sync", t_kbt)
            p.wait("tensor", t_kbt)
            p.wait("vector", t_kbt)
            p.wait("scalar", t_kbt)
        pcz_free = t_ctx
        pst_free = [t_bzc, t_bzc]
        pT_free = [t_pv_last, t_pv_last]

    psfree("pst1", "pst0", "pz", "pctx", "pq")
    attn_sb = [sb(f"attn_sb{i}", [P, D], BF16) for i in range(2)]
    pout = ps("pout", [P, D], F32)

    attn_sb_free = [ZERO, ZERO]
    pout_free = ZERO
    t_half0 = ZERO
    t_wo = t_wo0
    for tok in tmax(*ctx_done):
        p.wait("tensor", tok)
    for half in range(2):
        if half == 1:
            p.wait("gpsimd", t_half0)
            t_wo = p.dma("gpsimd", wobf.ap().rearrange("p (j c) -> p j c", j=8),
                         wo_hbm[1024:2048, :].rearrange("(j p) c -> p j c", p=P), "dwo")
        for a in range(8):
            bi = a % 2
            p.wait("tensor", t_wo)
            p.wait("tensor", pout_free)
            t_mm = ZERO
            for j in range(8):
                jg = half * 8 + j
                for nn in range(4):
                    t_mm = p.do("tensor", lambda e, jj=jg, aa=a, lj=j, n0=nn * 512: e.matmul(
                        pout.ap()[:, n0:n0 + 512],
                        ctxAll.ap()[:, jj * 1024 + aa * P: jj * 1024 + (aa + 1) * P],
                        wobf.ap()[:, lj * D + n0: lj * D + n0 + 512],
                        start=(lj == 0), stop=(lj == 7)),
                        inc=(SPE if (j == 7 and nn == 3) else None))
            if half == 0 and a == 7:
                t_half0 = t_mm
            if half == 0:
                p.wait("scalar", t_mm)
                p.wait("scalar", attn_sb_free[bi])
                pout_free = p.do("scalar", lambda e, b=bi: e.activation(
                    attn_sb[b].ap()[:, :], pout.ap()[:, :], AF.Copy), inc=SACT,
                    flush=True)
                p.wait("sync", pout_free)
                attn_sb_free[bi] = p.dma("sync", attn_dram[a * P:(a + 1) * P, :],
                                         attn_sb[bi].ap()[:, :], f"datn{bi}")
            else:
                p.wait("sync", attn_sb_free[bi])
                t_ld = p.dma("sync", attn_sb[bi].ap()[:, :],
                             attn_dram[a * P:(a + 1) * P, :], f"datn{bi}")
                p.wait("vector", t_ld)
                p.wait("vector", t_mm)
                pout_free = p.do("vector", lambda e, b=bi: e.tensor_tensor(
                    attn_sb[b].ap()[:, :], pout.ap()[:, :], attn_sb[b].ap()[:, :],
                    op=OP.add), inc=SDV, flush=True)
                p.wait("sync", pout_free)
                attn_sb_free[bi] = p.dma("sync", attn_dram[a * P:(a + 1) * P, :],
                                         attn_sb[bi].ap()[:, :], f"datn{bi}")

    t_attn_done = attn_sb_free[:]
    if debug:
        for tok in tmax(*ctx_done):
            p.wait("sync", tok)
        p.dma("sync", dbg_k, kbuf.ap()[:, :], "ddbg")
        p.dma("sync", dbg_v, vbuf.ap()[:, :], "ddbg")
        p.dma("sync", dbg_ctx, ctxAll.ap()[:, :], "ddbg")
        t_dbg = p.dma("sync", dbg_qh[:, 1024:2048], qh_buf[1].ap()[:, :], "ddbg")
        p.wait("sync", t_dbg)
        p.wait("tensor", t_dbg)
        p.wait("vector", t_dbg)
        p.wait("scalar", t_dbg)
        p.wait("gpsimd", t_dbg)

    sbfree("attn_sb1", "attn_sb0", "bz_sb", "lnz_sb", "pT1", "pT0",
           "ctxAll", "qevac", "qh1", "qh0", "wqbf1", "wqbf0", "wobf")
    psfree("pout")
    sbfree("rope2", "rope1", "maskt_bf", "sinq_bf", "cosq_bf", "vbuf", "kbuf")

    rs_row = sb("rs_row", [1, 1024], F32)
    bcast_rs = sb("bcast_rs", [P, 1024], BF16)
    qscr = sb("qscr", [P, 1024], BF16)
    hT_all = sb("hT_all", [P, 44 * 1024], BF16)
    w1stg = [sb(f"w1stg{i}", [P, D], F32) for i in range(2)]
    w1deq = sb("w1deq", [P, D], BF16)
    w1t = [sb(f"w1t{i}", [P, 16 * 512], BF16) for i in range(2)]
    g_sb = sb("g_sb", [P, 512], F32)
    up_sb = sb("up_sb", [P, 512], F32)

    ptr8 = ps("ptr8", [1, 512], F32)
    psc = ps("psc", [P, 1024], F32)

    p.wait("tensor", (SDV, p.cnt[SDV]))
    p.wait("tensor", (SACT, p.cnt[SACT]))
    t_trc = ZERO
    for a in range(8):
        p.wait("tensor", xq_tok)
        p.wait("tensor", t_trc)
        t_tr = p.do("tensor", lambda e, aa=a: e.transpose(
            ptr8.ap()[0:1, 0:P], sxq_rscale.ap()[:, aa:aa + 1], ident_f.ap()[:, :]),
            inc=SPE)
        p.wait("vector", t_tr)
        t_trc = p.do("vector", lambda e, aa=a: e.tensor_copy(
            rs_row.ap()[0:1, aa * P:(aa + 1) * P], ptr8.ap()[0:1, 0:P]), inc=SDV,
            flush=True)
    p.wait("tensor", t_trc)
    p.do("tensor", lambda e: e.matmul(
        psc.ap()[:, 0:512], onesr.ap()[:, :], rs_row.ap()[:, 0:512],
        start=True, stop=True))
    t_bc = p.do("tensor", lambda e: e.matmul(
        psc.ap()[:, 512:1024], onesr.ap()[:, :], rs_row.ap()[:, 512:1024],
        start=True, stop=True), inc=SPE)
    p.wait("vector", t_bc)
    t_bcc = p.do("vector", lambda e: e.tensor_copy(bcast_rs.ap()[:, :], psc.ap()[:, :]), inc=SDV, flush=True)
    t_qx = ZERO
    for j in range(16):
        p.wait("vector", t_bcc)
        p.wait("vector", qproj_done[NQH - 1])
        p.do("vector", lambda e, jj=j: e.tensor_tensor(
            qscr.ap()[:, :], xnT_own.ap()[:, jj * 1024:(jj + 1) * 1024],
            bcast_rs.ap()[:, :], op=OP.mult), flush=True)
        t_qx = p.do("vector", lambda e, jj=j: e._custom_dve(
            ROUND_OP, out=xnT_own.ap()[:, jj * 1024:(jj + 1) * 1024],
            in0=qscr.ap()[:, :], s0=1.0, s1=1.0, imm2=MAGIC), inc=SDV, flush=True)

    psfree("psc", "ptr8")
    if debug:
        p.wait("sync", t_qx)
        p.dma("sync", dbg_xnq, xnT_own.ap()[:, :], "ddbg")
        p.dma("sync", dbg_sc[:, 0:8], sxq_scale.ap()[:, :], "ddbg")
        t_dbg2 = p.dma("sync", dbg_sc[:, 8:16], sxq_rscale.ap()[:, :], "ddbg")
        p.wait("sync", t_dbg2)
        p.wait("tensor", t_dbg2)
        p.wait("vector", t_dbg2)
        p.wait("scalar", t_dbg2)

    pgu = [ps(f"pgu{i}", [P, 512], F32) for i in range(4)]
    ptw = ps("ptw", [P, D], BF16)
    pth = ps("pth", [P, 512], BF16)

    w1stg_free = [ZERO, ZERO]
    w1deq_free = ZERO
    ptw_free = ZERO
    w1t_free = [ZERO, ZERO]
    pgu_free = [ZERO, ZERO]
    pth_free = ZERO

    p.do("gpsimd", lambda e: e.memset(shq_amax.ap()[:, :], 0.0), inc=SG)
    t_shq0 = (SG, p.cnt[SG])
    p.do("gpsimd", lambda e: e.memset(w2_amax.ap()[:, :], 0.0), inc=SG)
    t_w2a0 = (SG, p.cnt[SG])
    w2p_list = [(ot, pc) for ot in range(16) for pc in range(3)]
    w2p_widths = [2048, 2048, 1536]
    w2p_idx = 0

    for tok in tmax(*t_attn_done):
        p.wait("sync", tok)
    p.wait("sync", (SPE, p.cnt[SPE]))
    p.wait("sync", (SDV, p.cnt[SDV]))
    p.wait("sync", (SACT, p.cnt[SACT]))

    si = 0
    for g in range(11):
        t_w1t_ready = ZERO
        for gu in range(2):
            for ot in range(4):
                bi = si % 2
                base = gu * I_FFN + g * 512
                p.wait("sync", w1stg_free[bi])
                t_w1 = p.dma("sync", w1stg[bi].ap()[:, :],
                             w1_hbm[base + ot * P: base + ot * P + P, :], f"dw1{bi}")
                p.wait("vector", t_w1)
                p.do("vector", lambda e, b=bi: e.tensor_reduce(
                    amax_t.ap()[:, :], w1stg[b].ap()[:, :], axis=AX.X, op=OP.max,
                    apply_absolute_value=True), flush=True)
                p.do("vector", lambda e: e.tensor_scalar(
                    wsc_t.ap()[:, :], amax_t.ap()[:, :], 1.0 / 127.0, 1e-8,
                    op0=OP.mult, op1=OP.max), flush=True)
                p.do("vector", lambda e: e.reciprocal(wrs_t.ap()[:, :], wsc_t.ap()[:, :]),
                     flush=True)
                p.wait("vector", w1deq_free)
                t_dq = p.do("vector", lambda e, b=bi: e._custom_dve(
                    ROUND_OP, out=w1deq.ap()[:, :], in0=w1stg[b].ap()[:, :],
                    s0=wrs_t.ap()[:, :], s1=wsc_t.ap()[:, :], imm2=MAGIC), inc=SDV,
                    flush=True)
                w1stg_free[bi] = t_dq
                p.wait("tensor", t_dq)
                p.wait("tensor", ptw_free)
                t_t = ZERO
                for j in range(16):
                    t_t = p.do("tensor", lambda e, jj=j: e.transpose(
                        ptw.ap()[:, jj * P:(jj + 1) * P],
                        w1deq.ap()[:, jj * P:(jj + 1) * P], ident_bf.ap()[:, :]),
                        inc=(SPE if j == 15 else None))
                w1deq_free = t_t
                p.wait("vector", t_t)
                p.wait("vector", w1t_free[gu])
                dstw = w1t[gu].ap().rearrange("p (j c) -> p j c", j=16)[:, :, ot * P:(ot + 1) * P]
                ptw_free = p.do("vector", lambda e, d=dstw: e.tensor_copy(
                    d, ptw.ap().rearrange("p (j c) -> p j c", j=16)), inc=SDV,
                    flush=True)
                t_w1t_ready = ptw_free
                si += 1
        for a in range(8):
            bi = a % 2
            p.wait("tensor", t_w1t_ready)
            p.wait("tensor", t_qx)
            p.wait("tensor", pgu_free[bi])
            t_mm = ZERO
            for gu in range(2):
                for j in range(16):
                    t_mm = p.do("tensor", lambda e, jj=j, aa=a, b=bi, w=gu: e.matmul(
                        pgu[2 * b + w].ap()[:, :],
                        xnT_own.ap()[:, jj * 1024 + aa * P: jj * 1024 + (aa + 1) * P],
                        w1t[w].ap()[:, jj * 512:(jj + 1) * 512],
                        start=(jj == 0), stop=(jj == 15)),
                        inc=(SPE if (j == 15 and gu == 1) else None))
            last_gu_mm = t_mm
            p.wait("scalar", t_mm)
            t_g = p.do("scalar", lambda e, b=bi, aa=a: e.activation(
                g_sb.ap()[:, :], pgu[2 * b].ap()[:, :], AF.Silu,
                scale=sxq_scale.ap()[:, aa:aa + 1]), inc=SACT, flush=True)
            p.wait("vector", t_g)
            p.do("vector", lambda e, b=bi, aa=a: e.tensor_scalar(
                up_sb.ap()[:, :], pgu[2 * b + 1].ap()[:, :],
                sxq_scale.ap()[:, aa:aa + 1], None, op0=OP.mult))
            t_h = p.do("vector", lambda e: e.tensor_tensor(
                g_sb.ap()[:, :], g_sb.ap()[:, :], up_sb.ap()[:, :], op=OP.mult), inc=SDV)
            p.wait("vector", t_shq0)
            p.do("vector", lambda e, aa=a: e.tensor_reduce(
                amax_t.ap()[:, :], g_sb.ap()[:, :], axis=AX.X, op=OP.max,
                apply_absolute_value=True), flush=True)
            p.do("vector", lambda e, aa=a: e.tensor_tensor(
                shq_amax.ap()[:, aa:aa + 1], shq_amax.ap()[:, aa:aa + 1],
                amax_t.ap()[:, :], op=OP.max), flush=True)
            t_hbf = p.do("vector", lambda e: e.tensor_copy(
                qscr.ap()[:, 0:512], g_sb.ap()[:, :]), inc=SDV, flush=True)
            pgu_free[bi] = t_hbf
            p.wait("tensor", t_hbf)
            p.wait("tensor", pth_free)
            t_th = ZERO
            for q in range(4):
                t_th = p.do("tensor", lambda e, qq=q: e.transpose(
                    pth.ap()[:, qq * P:(qq + 1) * P],
                    qscr.ap()[:, qq * P:(qq + 1) * P], ident_bf.ap()[:, :]),
                    inc=(SPE if q == 3 else None))
            p.wait("vector", t_th)
            dsth = hT_all.ap().rearrange("p (ic c) -> p ic c", ic=44)[
                :, 4 * g:4 * g + 4, a * P:(a + 1) * P]
            pth_free = p.do("vector", lambda e, d=dsth: e.tensor_copy(
                d, pth.ap().rearrange("p (q c) -> p q c", q=4)), inc=SDV, flush=True)
        w1t_free = [last_gu_mm, last_gu_mm]
        n_this = 5 if g < 9 else (48 - w2p_idx) // (11 - g) if g < 10 else 48 - w2p_idx
        for _ in range(min(n_this, 48 - w2p_idx)):
            ot, pc = w2p_list[w2p_idx]; w2p_idx += 1
            wdt = w2p_widths[pc]
            bi2 = si % 2; si += 1
            p.wait("sync", w1stg_free[bi2])
            t_p1 = p.dma("sync", w1stg[bi2].ap()[:, 0:wdt],
                         w2_hbm[ot * P:(ot + 1) * P, pc * 2048:pc * 2048 + wdt], f"dw1{bi2}")
            p.wait("vector", t_p1)
            p.wait("vector", t_w2a0)
            p.do("vector", lambda e, b=bi2, w=wdt: e.tensor_reduce(
                amax_t.ap()[:, :], w1stg[b].ap()[:, 0:w], axis=AX.X, op=OP.max,
                apply_absolute_value=True), flush=True)
            t_up = p.do("vector", lambda e, o=ot: e.tensor_tensor(
                w2_amax.ap()[:, o:o + 1], w2_amax.ap()[:, o:o + 1],
                amax_t.ap()[:, :], op=OP.max), inc=SDV, flush=True)
            w1stg_free[bi2] = t_up

    t_w2sc = ZERO
    for o in range(16):
        p.do("vector", lambda e, oo=o: e.tensor_scalar(
            w2_sc.ap()[:, oo:oo + 1], w2_amax.ap()[:, oo:oo + 1],
            1.0 / 127.0, 1e-8, op0=OP.mult, op1=OP.max), flush=True)
        t_w2sc = p.do("vector", lambda e, oo=o: e.reciprocal(
            w2_rs.ap()[:, oo:oo + 1], w2_sc.ap()[:, oo:oo + 1]), inc=SDV, flush=True)

    sbfree("up_sb", "g_sb", "w1t1", "w1t0", "w1deq", "w1stg1", "w1stg0")
    sbfree("xnT_own")

    psfree("pth", "ptw", "pgu3", "pgu2", "pgu1", "pgu0")
    t_hT_done = pth_free

    rsh_row = rs_row
    bcast_h = bcast_rs
    hscr = qscr
    W2CW = 11 * P
    w2stg = [sb(f"w2stg{i}", [P, W2CW], F32) for i in range(2)]
    w2deq = sb("w2deq", [P, W2CW], BF16)
    w2t = sb("w2t", [P, 44 * 512], BF16)
    xres = [sb(f"xres{i}", [P, 512], F32) for i in range(2)]
    attld = [sb(f"attld{i}", [P, 512], BF16) for i in range(2)]
    out_sb = [sb(f"out_sb{i}", [P, 512], F32) for i in range(2)]
    tmp_sb = sb("tmp_sb", [P, 512], F32)

    ptr8b = ps("ptr8b", [1, 512], F32)
    pscb = ps("pscb", [P, 1024], F32)

    p.wait("vector", t_hT_done)
    t_sh = ZERO
    for a in range(8):
        p.do("vector", lambda e, aa=a: e.tensor_scalar(
            shq_scale.ap()[:, aa:aa + 1], shq_amax.ap()[:, aa:aa + 1],
            1.0 / 127.0, 1e-8, op0=OP.mult, op1=OP.max), flush=True)
        t_sh = p.do("vector", lambda e, aa=a: e.reciprocal(
            shq_rscale.ap()[:, aa:aa + 1], shq_scale.ap()[:, aa:aa + 1]), inc=SDV,
            flush=True)
    t_trc = ZERO
    for a in range(8):
        p.wait("tensor", t_sh)
        p.wait("tensor", t_trc)
        t_tr = p.do("tensor", lambda e, aa=a: e.transpose(
            ptr8b.ap()[0:1, 0:P], shq_rscale.ap()[:, aa:aa + 1], ident_f.ap()[:, :]),
            inc=SPE)
        p.wait("vector", t_tr)
        t_trc = p.do("vector", lambda e, aa=a: e.tensor_copy(
            rsh_row.ap()[0:1, aa * P:(aa + 1) * P], ptr8b.ap()[0:1, 0:P]), inc=SDV,
            flush=True)
    p.wait("tensor", t_trc)
    p.do("tensor", lambda e: e.matmul(
        pscb.ap()[:, 0:512], onesr.ap()[:, :], rsh_row.ap()[:, 0:512],
        start=True, stop=True))
    t_bc = p.do("tensor", lambda e: e.matmul(
        pscb.ap()[:, 512:1024], onesr.ap()[:, :], rsh_row.ap()[:, 512:1024],
        start=True, stop=True), inc=SPE)
    p.wait("vector", t_bc)
    t_bcc = p.do("vector", lambda e: e.tensor_copy(bcast_h.ap()[:, :], pscb.ap()[:, :]), inc=SDV, flush=True)
    t_qh = ZERO
    for ic in range(44):
        p.wait("vector", t_bcc)
        p.do("vector", lambda e, jj=ic: e.tensor_tensor(
            hscr.ap()[:, :], hT_all.ap()[:, jj * 1024:(jj + 1) * 1024],
            bcast_h.ap()[:, :], op=OP.mult), flush=True)
        t_qh = p.do("vector", lambda e, jj=ic: e._custom_dve(
            ROUND_OP, out=hT_all.ap()[:, jj * 1024:(jj + 1) * 1024],
            in0=hscr.ap()[:, :], s0=1.0, s1=1.0, imm2=MAGIC), inc=SDV, flush=True)

    if debug:
        p.wait("sync", t_qh)
        p.dma("sync", dbg_hT, hT_all.ap()[:, :], "ddbg")
        p.dma("sync", dbg_sc[:, 16:24], shq_scale.ap()[:, :], "ddbg")
        t_dbg3 = p.dma("sync", dbg_sc[:, 24:32], shq_rscale.ap()[:, :], "ddbg")
        p.wait("sync", t_dbg3)
        p.wait("tensor", t_dbg3)
        p.wait("vector", t_dbg3)
        p.wait("scalar", t_dbg3)

    psfree("pscb", "ptr8b")
    pdn = [ps(f"pdn{i}", [P, 512], F32) for i in range(4)]
    ptw2 = ps("ptw2", [P, D], BF16)

    w2stg_free = [ZERO, ZERO]
    w2deq_free = ZERO
    ptw2_free = ZERO
    w2t_free = ZERO
    pdn_free = [ZERO] * 4
    xres_free = [ZERO, ZERO]
    attld_free = [ZERO, ZERO]
    out_free = [ZERO, ZERO]
    t_out = []

    p.wait("sync", (SPE, p.cnt[SPE]))
    p.wait("sync", (SDV, p.cnt[SDV]))
    p.wait("sync", (SACT, p.cnt[SACT]))
    for oc in range(4):
        t_w2t_ready = ZERO
        for ot in range(4):
            o_glob = oc * 4 + ot
            for ib in range(4):
                bi = (ot * 4 + ib) % 2
                p.wait("sync", w2stg_free[bi])
                t_w2 = p.dma("sync", w2stg[bi].ap()[:, :],
                             w2_hbm[oc * 512 + ot * P: oc * 512 + (ot + 1) * P,
                                    ib * W2CW:(ib + 1) * W2CW], f"dw2{bi}")
                p.wait("vector", t_w2)
                p.wait("vector", t_w2sc)
                p.wait("vector", w2deq_free)
                t_dq = p.do("vector", lambda e, b=bi, o=o_glob: e._custom_dve(
                    ROUND_OP, out=w2deq.ap()[:, :], in0=w2stg[b].ap()[:, :],
                    s0=w2_rs.ap()[:, o:o + 1], s1=w2_sc.ap()[:, o:o + 1],
                    imm2=MAGIC), inc=SDV, flush=True)
                w2stg_free[bi] = t_dq
                p.wait("tensor", t_dq)
                p.wait("tensor", w2t_free)
                p.wait("tensor", ptw2_free)
                t_t = ZERO
                for q in range(11):
                    t_t = p.do("tensor", lambda e, qq=q: e.transpose(
                        ptw2.ap()[:, qq * P:(qq + 1) * P],
                        w2deq.ap()[:, qq * P:(qq + 1) * P], ident_bf.ap()[:, :]),
                        inc=(SPE if q == 10 else None))
                w2deq_free = t_t
                p.wait("vector", t_t)
                dstw = w2t.ap().rearrange("p (ic c) -> p ic c", ic=44)[
                    :, ib * 11:(ib + 1) * 11, ot * P:(ot + 1) * P]
                srcw = ptw2.ap().rearrange("p (j c) -> p j c", j=16)[:, 0:11, :]
                ptw2_free = p.do("vector", lambda e, d=dstw, s=srcw: e.tensor_copy(d, s),
                                 inc=SDV, flush=True)
                t_w2t_ready = ptw2_free
        t_dn_done = [ZERO] * 8
        for hf in range(2):
            for ia, a in enumerate(range(hf * 4, hf * 4 + 4)):
                p.wait("tensor", t_w2t_ready)
                p.wait("tensor", t_qh)
                p.wait("tensor", pdn_free[ia])
                t_mm = ZERO
                for ic in range(44):
                    t_mm = p.do("tensor", lambda e, jj=ic, aa=a, i=ia: e.matmul(
                        pdn[i].ap()[:, :],
                        hT_all.ap()[:, jj * 1024 + aa * P: jj * 1024 + (aa + 1) * P],
                        w2t.ap()[:, jj * 512:(jj + 1) * 512],
                        start=(jj == 0), stop=(jj == 43)),
                        inc=(SPE if ic == 43 else None))
                t_dn_done[a] = t_mm
            for ia, a in enumerate(range(hf * 4, hf * 4 + 4)):
                bi = a % 2
                p.wait("sync", xres_free[bi])
                t_xr = p.dma("sync", xres[bi].ap()[:, :],
                             x_hbm[a * 512:a * 512 + P, oc * 512:(oc + 1) * 512], f"dxr{bi}")
                p.wait("sync", attld_free[bi])
                for tok in tmax(*t_attn_done):
                    p.wait("sync", tok)
                t_al = p.dma("sync", attld[bi].ap()[:, :],
                             attn_dram[a * P:(a + 1) * P, oc * 512:(oc + 1) * 512], f"datl{bi}")
                p.wait("vector", t_dn_done[a])
                p.wait("vector", t_al)
                p.do("vector", lambda e, i=ia, aa=a, b=bi: e.scalar_tensor_tensor(
                    tmp_sb.ap()[:, :], pdn[i].ap()[:, :],
                    shq_scale.ap()[:, aa:aa + 1], attld[b].ap()[:, :],
                    op0=OP.mult, op1=OP.add), flush=True)
                p.wait("vector", t_xr)
                p.wait("vector", out_free[bi])
                t_o = p.do("vector", lambda e, b=bi: e.tensor_tensor(
                    out_sb[b].ap()[:, :], tmp_sb.ap()[:, :], xres[b].ap()[:, :],
                    op=OP.add), inc=SDV, flush=True)
                pdn_free[ia] = t_o
                xres_free[bi] = t_o
                attld_free[bi] = t_o
                p.wait("sync", t_o)
                t_st = p.dma("sync", out_hbm[a * P:(a + 1) * P, oc * 512:(oc + 1) * 512],
                             out_sb[bi].ap()[:, :], SOUT)
                out_free[bi] = t_st
                t_out.append(t_st)
        w2t_free = t_mm

    for tok in tmax(*t_out):
        p.wait("sync", tok)

    with nc.Block() as block:
        sem_handles = {}
        sem_mgrs = []
        for name in sorted(set(s for e in ENGS for item in p.plan[e]
                        for s in ([item[1]] if item[0] == "w" else ([item[2]] if item[2] else [])))):
            m = nc.semaphore(name)
            sem_handles[name] = m.__enter__()
            sem_mgrs.append(m)

        def emit(eng_name):
            def run(e):
                for item in p.plan[eng_name]:
                    if item[0] == "w":
                        e.wait_ge(sem_handles[item[1]], item[2])
                    else:
                        inst = item[1](e)
                        if item[2] is not None:
                            inst.then_inc(sem_handles[item[2]], item[3])
            return run
        block.sync(emit("sync"))
        block.scalar(emit("scalar"))
        block.vector(emit("vector"))
        block.tensor(emit("tensor"))
        block.gpsimd(emit("gpsimd"))

    mybir.codegen_inst_isa_subclasses(nc)
    return nc



def _host_prep(L0_attn_x, Wq, Wk, Wv, Wo, W1, W2):
    B = 2
    inv_freq = 1.0 / (10000.0 ** (np.arange(0, 64, dtype=np.float64) / 64.0))
    perm128 = np.concatenate([np.arange(0, 128, 2), np.arange(1, 128, 2)])
    qperm = np.concatenate([h * 128 + perm128 for h in range(16)])
    kperm = np.concatenate([h * 128 + perm128 for h in range(4)])
    Wq_p = np.ascontiguousarray(Wq[:, qperm])
    Wk_p = np.ascontiguousarray(Wk[:, kperm])

    in_maps = []
    metas = []
    for c in range(8):
        b, sg = c // 4, c % 4
        tiles = []
        for a in range(8):
            tl = [4 * a + sg] + [4 * a + u for u in range(4) if u != sg]
            tiles.append(tl)
        order = [t for tl in tiles for t in tl]
        xb = np.ascontiguousarray(
            L0_attn_x[b].reshape(32, P, D)[order].reshape(S, D))
        pos = np.concatenate([np.arange(t * P, (t + 1) * P) for t in order]).astype(np.float64)
        ang = pos[None, :] * inv_freq[:, None]
        cosv = np.cos(ang)
        sinv = np.sin(ang)
        cosd = np.vstack([cosv, cosv]).astype(np.float32)
        msin = np.vstack([-sinv, sinv]).astype(np.float32)
        m = np.zeros((P, 512), dtype=np.float32)
        kk = np.arange(P)[:, None]
        rr = np.arange(P)[None, :]
        m[:, 0:P] = (kk <= rr).astype(np.float32)
        others = [u for u in range(4) if u != sg]
        for i, u in enumerate(others):
            m[:, (i + 1) * P:(i + 2) * P] = 1.0 if u < sg else 0.0
        in_maps.append({
            "x": xb, "wq": Wq_p, "wk": Wk_p, "wv": np.ascontiguousarray(Wv),
            "wo": np.ascontiguousarray(Wo), "w1": np.ascontiguousarray(W1),
            "w2": np.ascontiguousarray(W2), "cosd": cosd, "msin": msin,
            "maskt": m, "ident": np.eye(P, dtype=np.float32),
        })
        metas.append((b, sg))
    return in_maps, metas


def kernel(L0_attn_x, L0_attn_Wq, L0_attn_Wk, L0_attn_Wv, L0_attn_Wo,
           L0_ffn_w_gate_up, L0_ffn_w_down):
    x = np.asarray(L0_attn_x, dtype=np.float32)
    in_maps, metas = _host_prep(
        x, np.asarray(L0_attn_Wq, np.float32), np.asarray(L0_attn_Wk, np.float32),
        np.asarray(L0_attn_Wv, np.float32), np.asarray(L0_attn_Wo, np.float32),
        np.asarray(L0_ffn_w_gate_up, np.float32), np.asarray(L0_ffn_w_down, np.float32))
    nc = build_program()
    res = run_bass_kernel_spmd(nc, in_maps, core_ids=list(range(8)))
    out = np.empty((2, S, D), dtype=np.float32)
    for c in range(8):
        b, sg = metas[c]
        oc = np.asarray(res.results[c]["out"], dtype=np.float32)
        for a in range(8):
            t = 4 * a + sg
            out[b, t * P:(t + 1) * P, :] = oc[a * P:(a + 1) * P, :]
    return out
